# version 12
# baseline (speedup 1.0000x reference)
import numpy as np
import ml_dtypes
import concourse.bass as bass
import concourse.mybir as mybir
from concourse.bass_utils import run_bass_kernel_spmd
from contextlib import ExitStack

F32 = mybir.dt.float32
BF16 = mybir.dt.bfloat16
FP8 = mybir.dt.float8e4
AX = mybir.AxisListType.X
ALU = mybir.AluOpType
ACT = mybir.ActivationFunctionType

NCORES = 8
T_CORE = 4096
NT = 32
D = 2048
KC = 16
NE = 16
TOPK = 2048.0
BISECT_ITERS = 34
EPS = 1e-5
SEQT = 16
W8 = 256
WREG = 16384

DEV_TAPS = False


class Prog:

    DMA_SEMS = ("dx", "dg", "cc")

    def __init__(self):
        self.cnt = {"dx": 0, "dg": 0, "cc": 0, "ts": 0, "vs": 0, "ss": 0, "gs": 0}
        self.step = {"dx": 16, "dg": 16, "cc": 1, "ts": 1, "vs": 1, "ss": 1, "gs": 1}
        self.events = []
        self.floor = {}

    def ev(self, eng, fn, waits=(), inc=None):
        ws = []
        seen = {}
        for tk in _flat(waits):
            sem, val, peng = tk
            if peng == eng and sem not in self.DMA_SEMS:
                continue
            if seen.get(sem, -1) < val:
                seen[sem] = val
        for sem, val in seen.items():
            if self.floor.get((eng, sem), -1) >= val:
                continue
            self.floor[(eng, sem)] = val
            ws.append((sem, val))
        incs = []
        tok = None
        if inc is not None:
            self.cnt[inc] += self.step[inc]
            incs.append((inc, self.step[inc]))
            tok = (inc, self.cnt[inc], eng)
            if inc not in self.DMA_SEMS:
                self.floor[(eng, inc)] = self.cnt[inc]
        self.events.append((eng, ws, fn, incs))
        return tok


def _flat(x):
    if x is None:
        return
    if isinstance(x, tuple) and len(x) == 3 and isinstance(x[0], str):
        yield x
        return
    for it in x:
        yield from _flat(it)


def build_nc():
    nc = bass.Bass("TRN2", target_bir_lowering=False, debug=False)
    io = "ExternalOutput"

    x_in = nc.dram_tensor("x", [T_CORE, D], F32, kind="ExternalInput").ap()
    w_in = {n: nc.dram_tensor(n, [D, D], F32, kind="ExternalInput").ap()
            for n in ("wq", "wk", "wv", "wg", "we")}
    rw_in = nc.dram_tensor("rw", [D, NE], F32, kind="ExternalInput").ap()
    b_in = {n: nc.dram_tensor(n, [1, D], F32, kind="ExternalInput").ap()
            for n in ("bq", "bk", "bv", "bg", "be")}
    rb_in = nc.dram_tensor("rb", [1, NE], F32, kind="ExternalInput").ap()
    id_bf_in = nc.dram_tensor("id_bf", [128, 128], BF16, kind="ExternalInput").ap()
    id_f_in = nc.dram_tensor("id_f", [128, 128], F32, kind="ExternalInput").ap()
    tri_in = nc.dram_tensor("tri", [128, 128], BF16, kind="ExternalInput").ap()
    ones_in = nc.dram_tensor("ones1", [1, 128], F32, kind="ExternalInput").ap()
    gmat_in = nc.dram_tensor("gmat", [128, 128], F32, kind="ExternalInput").ap()
    we_bf = nc.dram_tensor("we_bf", [128, KC * D], BF16, kind="ExternalOutput").ap()
    y_out = nc.dram_tensor("y", [T_CORE, D], F32, kind="ExternalOutput").ap()

    xfT_sp = [nc.dram_tensor(f"xf{i}T_sp", [128, NT * D], BF16, kind=io).ap()
              for i in (1, 2, 3)]
    z1_sp = nc.dram_tensor("z1_sp", [T_CORE, D], BF16, kind=io).ap()
    x2_sp = nc.dram_tensor("x2_sp", [T_CORE, D], BF16, kind=io).ap()
    x3_sp = nc.dram_tensor("x3_sp", [T_CORE, D], BF16, kind=io).ap()
    z2_sp = nc.dram_tensor("z2_sp", [T_CORE, D], BF16, kind=io).ap()
    cc_in = [nc.dram_tensor(f"cc{i}_in", [NE, T_CORE], F32).ap() for i in (1, 2)]
    cc_out = [nc.dram_tensor(f"cc{i}_out", [128, T_CORE], F32).ap() for i in (1, 2)]
    cnt_d = nc.dram_tensor("cnt_d", [16, 8], F32).ap()
    dbg = (nc.dram_tensor("dbg", [128, 4096 + 512 + 32 + 32 + 16], F32,
                          kind="ExternalOutput").ap() if DEV_TAPS else None)
    th_d = nc.dram_tensor("th_d", [16, 1], F32).ap()

    ctx = ExitStack()
    sb = lambda name, shape, dt: ctx.enter_context(nc.sbuf_tensor(name, shape, dt))

    arena = sb("arena", [128, 2 * WREG], BF16)
    xin = [sb(f"xin{i}", [128, D], F32) for i in (0, 1)]
    bigf = [sb(f"bigf{i}", [128, D], F32) for i in (0, 1)]
    x3rA = [sb(f"x3rA{i}", [128, D], BF16) for i in (0, 1)]
    x3rB = [sb(f"x3rB{i}", [128, D], BF16) for i in (0, 1)]
    zr = [sb(f"zr{i}", [128, D], BF16) for i in (0, 1)]
    xf = [sb(f"xf{i}", [128, D], BF16) for i in (0, 1)]
    xfT = [sb(f"xfT{i}", [128, D], BF16) for i in (0, 1)]
    xfTr = [sb(f"xfTr{i}", [128, D], BF16) for i in (0, 1)]
    zev = [sb(f"zev{i}", [128, 512], BF16) for i in range(4)]
    junk = sb("junk", [128, 4096], FP8)
    junk_bf = junk.bitcast(BF16)[:, 0:D]
    Sg = sb("Sg", [128, T_CORE], F32)
    S1tm = sb("S1tm", [128, NT * NE], F32)
    S2tm = sb("S2tm", [128, NT * NE], F32)
    sxev = [sb(f"sxev{i}", [16, 128], F32) for i in (0, 1)]
    grp = sb("grp", [128, 4096], F32)
    beb = sb("beb", [128, D], BF16)
    small = sb("small", [128, 2048], F32)
    id_bf = sb("id_bf_sb", [128, 128], BF16)
    id_f = sb("id_f_sb", [128, 128], F32)
    tri = sb("tri_sb", [128, 128], BF16)
    ones1 = sb("ones1_sb", [1, 128], F32)
    gmat = sb("gmat_sb", [128, 128], F32)

    rw_sb = sb("rw_sb", [128, KC * NE], BF16)
    rbb = sb("rbb", [128, NE], F32)

    gb = grp.bitcast(BF16)
    x2q = [gb[:, 0:256], gb[:, 256:512]]
    kvb = gb[:, 512:768]
    attb = {w: [gb[:, 768 + (2 * wi + d) * 256: 768 + (2 * wi + d + 1) * 256] for d in (0, 1)]
            for wi, w in enumerate(("bq", "bk", "bv", "bg"))}
    x3q = [grp[:, 1408:1664], grp[:, 1664:1920]]
    v_sb = grp[:, 1920:2176]
    k_sb = grp[:, 2176:2432]
    g_sb = grp[:, 2432:2688]
    q_sb = grp[:, 2688:2944]
    sig_sb = grp[:, 2944:3200]
    t1_sb = grp[:, 3200:3456]
    carrst = grp[:, 3456:3712]
    carryq = grp[:, 3712:3968]
    b_lo = grp[:, 3968:3969]
    b_hi = grp[:, 3969:3970]
    b_mid = grp[:, 3970:3971]
    b_cntp = grp[:, 3971:3972]
    b_cntr = grp[:, 3972:3973]
    b_sel = grp.bitcast(mybir.dt.int32)[:, 3973:3974]
    b_cnt8 = grp[:, 3976:3984]
    thbc = grp[:, 3984:4000]

    def scol(off, t=0):
        return small[:, off + t: off + t + 1]
    ST = {"sum": 0, "sq": 32, "m": 64, "var": 96, "rstd": 128, "mb": 160}
    LNB = {1: 0, 2: 192, 3: 384}
    s3sum = small[:, 576:832]
    s3sq = small[:, 832:1088]
    s1c = small[:, 1088:1120]
    hs1 = small[:, 1120:1152]
    s2c = small[:, 1152:1184]
    hs2 = small[:, 1184:1216]
    MX, MXN, DEN, RDEN = 1216, 1220, 1224, 1228
    c_eps = small[:, 1232:1233]
    c_nh = small[:, 1233:1234]
    lgf = [small[:, 1344:1360], small[:, 1360:1376]]
    Ebuf = [small[:, 1376:1392], small[:, 1392:1408]]
    small_mask = small[:, 1472:1984]

    ps_z = [ctx.enter_context(nc.psum_tensor(f"psz{i}", [128, 512], F32)) for i in range(4)]
    ps_c = ctx.enter_context(nc.psum_tensor("psc", [128, 512], F32))
    ps_tr = [ctx.enter_context(nc.psum_tensor(f"pst{i}", [128, 1024], BF16)) for i in (0, 1)]
    ps_ls = ctx.enter_context(nc.psum_tensor("psls", [128, 512], F32))
    ps_lg = ps_ls[:, 0:16]
    ps_cnt = ps_ls[:, 256:257]
    ps_sx = ps_ls[0:16, 128:256]

    sems = {k: ctx.enter_context(nc.semaphore(k))
            for k in ("dx", "dg", "cc", "ts", "vs", "ss", "gs")}
    P = Prog()
    free = {}
    F = free.get

    def vnop(n=3):
        for _ in range(n):
            P.ev("vector", lambda e: e.engine_nop(), [])

    def dma(queue, dst, src, waits=(), slow=False):
        sem = "dg" if queue == "gpsimd" else "dx"
        return P.ev(queue, (lambda e, d=dst, s=src, sl=slow:
                            e.dma_start(out=d, in_=s, allow_slow_non_contiguous=sl)),
                    waits, inc=sem)

    t_cst = None
    ps_ls_chain = [None]

    def stats_norm(t, xsrc, base, src_tok, out_xf_t):
        p = t % 2
        tv = P.ev("vector", lambda e, o=scol(ST["sum"] + base, t), i=xsrc:
                  e.reduce_sum(o, i, axis=AX), [src_tok], inc="vs")
        ts_ = P.ev("scalar", lambda e, i=xsrc, a=scol(ST["sq"] + base, t):
                   e.activation(out=junk_bf, in_=i, func=ACT.Square, accum_out=a),
                   [src_tok, F("junk")], inc="ss")
        vnop()
        tvm1 = P.ev("vector", lambda e, o=scol(ST["m"] + base, t), i=scol(ST["sum"] + base, t):
                    e.tensor_scalar_mul(o, i, 1.0 / D), [tv], inc="vs")
        tsq2 = P.ev("scalar", lambda e, o=scol(ST["var"] + base, t), i=scol(ST["m"] + base, t):
                    e.activation(out=o, in_=i, func=ACT.Square), [tvm1], inc="ss")
        tv2 = P.ev("vector", lambda e, o=scol(ST["var"] + base, t), i=scol(ST["sq"] + base, t):
                   e.scalar_tensor_tensor(out=o, in0=i, scalar=1.0 / D, in1=o,
                                          op0=ALU.mult, op1=ALU.subtract),
                   [ts_, tsq2], inc="vs")
        P.ev("scalar", lambda e, o=scol(ST["rstd"] + base, t), i=scol(ST["var"] + base, t):
             e.activation(out=o, in_=i, func=ACT.Ln, bias=c_eps), [tv2, t_cst])
        ts3 = P.ev("scalar", lambda e, o=scol(ST["rstd"] + base, t):
                   e.activation(out=o, in_=o, func=ACT.Exp, scale=c_nh), [], inc="ss")
        vnop(2)
        tv3 = P.ev("vector", lambda e, o=scol(ST["mb"] + base, t), m=scol(ST["m"] + base, t),
                   r=scol(ST["rstd"] + base, t):
                   e.scalar_tensor_tensor(out=o, in0=m, scalar=-1.0, in1=r,
                                          op0=ALU.mult, op1=ALU.mult), [ts3], inc="vs")
        tnorm = P.ev("scalar", lambda e, o=xf[p][:], i=xsrc, r=scol(ST["rstd"] + base, t),
                     b=scol(ST["mb"] + base, t):
                     e.activation(out=o, in_=i, func=ACT.Identity, scale=r, bias=b),
                     [tv3, F(f"xf{p}")], inc="ss")
        free["junk"] = ts_
        return tnorm

    def transpose_tile(t, tnorm):
        p = t % 2
        tev = None
        for g in range(4):
            tt = None
            for b4 in range(4):
                b = 4 * g + b4
                tt = P.ev("tensor",
                          lambda e, o=ps_tr[g % 2][:, 128 * b4:128 * (b4 + 1)],
                          i=xf[p][:, 128 * b:128 * (b + 1)]:
                          e.transpose(o, i, id_bf[:]),
                          [tnorm, t_idb, F(f"pstr{g % 2}")],
                          inc="ts" if b4 == 3 else None)
            tev = P.ev("vector", lambda e, o=xfT[p][:, 512 * g:512 * (g + 1)],
                       i=ps_tr[g % 2][:, 0:512]: e.tensor_copy(o, i),
                       [tt, F(f"xfT{p}") if g == 0 else None], inc="vs")
            free[f"pstr{g % 2}"] = tev
        return tev

    def router_softmax(t, tev, Stm, cc_ap):
        p = t % 2
        tr = None
        for k in range(KC):
            tr = P.ev("tensor", lambda e, l=xfT[p][:, 128 * k:128 * (k + 1)],
                      r=rw_sb[:, NE * k:NE * (k + 1)], s=(k == 0), st=(k == KC - 1):
                      e.matmul(ps_lg, l, r, start=s, stop=st),
                      [tev, t_rw, ps_ls_chain[0]] if k == 0 else [],
                      inc="ts" if k == KC - 1 else None)
        tva = P.ev("vector", lambda e, o=lgf[p]: e.tensor_add(o, ps_lg, rbb[:]),
                   [tr, t_rbb, F(f"lgf{p}")], inc="vs")
        vnop()
        P.ev("vector", lambda e, o=scol(MX, p), i=lgf[p]: e.reduce_max(o, i, axis=AX), [])
        vnop()
        tvm = P.ev("vector", lambda e, o=scol(MXN, p), i=scol(MX, p):
                   e.tensor_scalar_mul(o, i, -1.0), [], inc="vs")
        tse = P.ev("scalar", lambda e, o=Ebuf[p], i=lgf[p], b=scol(MXN, p):
                   e.activation(out=o, in_=i, func=ACT.Exp, bias=b),
                   [tvm, F(f"Ebuf{p}")], inc="ss")
        P.ev("vector", lambda e, o=scol(DEN, p), i=Ebuf[p]: e.reduce_sum(o, i, axis=AX), [tse])
        vnop()
        tvr = P.ev("vector", lambda e, o=scol(RDEN, p), i=scol(DEN, p): e.reciprocal(o, i),
                   [], inc="vs")
        tvs = P.ev("scalar", lambda e, o=Stm[:, NE * t:NE * (t + 1)], i=Ebuf[p], r=scol(RDEN, p):
                   e.activation(out=o, in_=i, func=ACT.Identity, scale=r, bias=0.0),
                   [tvr], inc="ss")
        free[f"lgf{p}"] = tse
        free[f"Ebuf{p}"] = tvs
        ps_ls_chain[0] = tva
        return tr, tvs

    def sexp_batch(Stm, cc_ap, t_last_stm):
        tcs = None
        for t in range(NT):
            p = t % 2
            tts = P.ev("tensor", lambda e, i=Stm[:, NE * t:NE * (t + 1)]:
                       e.transpose(ps_sx, i, id_f[:]),
                       [t_last_stm, t_idf, ps_ls_chain[0]], inc="ts")
            tvx = P.ev("vector", lambda e, o=sxev[p][:]: e.tensor_copy(o, ps_sx),
                       [tts, F(f"sxev{p}")], inc="vs")
            ps_ls_chain[0] = tvx
            tcs = dma("sync", cc_ap[0:16, 128 * t:128 * (t + 1)], sxev[p][:], [tvx])
            free[f"sxev{p}"] = tcs
        return tcs

    def z_phase(sp_src, tok_src, z_dst, we_toks):
        toks = [None] * NT
        for t in range(NT):
            p = t % 2
            tl = dma("sync", xfTr[p][:], sp_src[:, D * t:D * (t + 1)],
                     [tok_src[t], F(f"xfTr{p}")])
            tmm = [None] * 4
            for n in range(4):
                for k in range(KC):
                    tmm[n] = P.ev("tensor",
                                  lambda e, l=xfTr[p][:, 128 * k:128 * (k + 1)],
                                  r=arena[:, D * k + 512 * n: D * k + 512 * (n + 1)],
                                  o=ps_z[n][:], s=(k == 0), st=(k == KC - 1):
                                  e.matmul(o, l, r, start=s, stop=st),
                                  [tl, we_toks[k], F(f"psz{n}") if k == 0 else None],
                                  inc="ts" if k == KC - 1 else None)
            free[f"xfTr{p}"] = tmm[3]
            tws = None
            for n in range(4):
                tse = P.ev("scalar", lambda e, o=zev[n][:], i=ps_z[n][:]:
                           e.activation(out=o, in_=i, func=ACT.Copy),
                           [tmm[n], F(f"zev{n}")], inc="ss")
                free[f"psz{n}"] = tse
                tw = dma("sync", z_dst[128 * t:128 * (t + 1), 512 * n:512 * (n + 1)],
                         zev[n][:], [tse])
                free[f"zev{n}"] = tw
                tws = tw
            toks[t] = tws
        return toks

    def gather_bisect(idx, cc_wait):
        tcc = P.ev("gpsimd", lambda e, i=cc_in[idx], o=cc_out[idx]:
                   e.collective_compute("AllGather", ALU.bypass,
                                        replica_groups=[list(range(NCORES))],
                                        ins=[i.opt()], outs=[o.opt()]),
                   [cc_wait], inc="cc")
        tsg = dma("gpsimd", Sg[:], cc_out[idx], [tcc])
        tvi = P.ev("vector", lambda e: e.memset(b_lo, 0.0), [tsg, F("bis")], inc="vs")
        P.ev("vector", lambda e: e.memset(b_hi, 1.0), [])
        tprev = tvi
        for _ in range(BISECT_ITERS):
            P.ev("gpsimd", lambda e: e.tensor_add(b_mid, b_lo, b_hi), [tprev])
            tgm = P.ev("gpsimd", lambda e: e.tensor_scalar_mul(b_mid, b_mid, 0.5),
                       [], inc="gs")
            tvc = P.ev("vector", lambda e: e.tensor_scalar(
                out=junk[:], in0=Sg[:], scalar1=b_mid, scalar2=0.0,
                op0=ALU.is_ge, op1=ALU.add, accum_out=b_cntp),
                [F("junk"), tgm], inc="vs")
            free["junk"] = tvc
            tmm_g = P.ev("tensor", lambda e: e.matmul(ps_cnt, gmat[:], b_cntp,
                                                      start=True, stop=True),
                         [tvc, t_gm, ps_ls_chain[0]], inc="ts")
            ps_ls_chain[0] = tmm_g
            P.ev("vector", lambda e: e.tensor_single_scalar(b_sel, ps_cnt, TOPK, ALU.is_ge), [tmm_g])
            vnop(2)
            P.ev("vector", lambda e: e.select(b_lo, b_sel, b_mid, b_lo), [])
            tprev = P.ev("vector", lambda e: e.select(b_hi, b_sel, b_hi, b_mid), [], inc="vs")
        td3 = dma("gpsimd", th_d, b_lo[0:16, 0:1], [tprev])
        td4 = dma("gpsimd", thbc.rearrange("p e -> p e"),
                  th_d.rearrange("e one -> (e one)").unsqueeze(0).broadcast_to([128, 16]),
                  [td3, F("thbc")])
        free["bis"] = td4
        return td4

    def s_values(Stm, sc, hc, tth):
        thb = thbc.unsqueeze(1).broadcast_to([128, NT, NE])
        tk1 = P.ev("vector", lambda e, i=Stm[:]:
                   e.tensor_tensor(small_mask.rearrange("p (a b) -> p a b", b=NE),
                                   i.rearrange("p (a b) -> p a b", b=NE), thb, ALU.is_ge),
                   [tth], inc="vs")
        vnop()
        P.ev("vector", lambda e, i=Stm[:]: e.tensor_mul(small_mask, small_mask, i), [])
        vnop()
        tlast = None
        for t in range(NT):
            tlast = P.ev("vector", lambda e, o=sc[:, t:t + 1], i=small_mask[:, NE * t:NE * (t + 1)]:
                         e.reduce_sum(o, i, axis=AX), [], inc="vs")
        th = P.ev("gpsimd", lambda e: e.tensor_scalar_mul(hc, sc, 0.5), [tlast], inc="gs")
        free["thbc"] = tk1
        return th

    P.ev("vector", lambda e: e.memset(c_eps, EPS), [])
    t_cst = P.ev("vector", lambda e: e.memset(c_nh, -0.5), [], inc="vs")

    t_idb = dma("sync", id_bf[:], id_bf_in)
    t_idf = dma("sync", id_f[:], id_f_in)
    t_tri = dma("sync", tri[:], tri_in)
    t_one = dma("sync", ones1[:], ones_in)
    t_gm = dma("sync", gmat[:], gmat_in)

    t_rbb = dma("sync", rbb[:], rb_in.partition_broadcast(128))
    t_rw = None
    for i in range(4):
        src = rw_in[512 * i:512 * (i + 1), :].rearrange("(c p) e -> p c e", p=128)
        dst = rw_sb[:, 64 * i:64 * (i + 1)].rearrange("p (c e) -> p c e", e=NE)
        t_rw = dma("gpsimd", dst, src)
    t_beb = dma("gpsimd", beb[:], b_in["be"].partition_broadcast(128))
    t_we = [dma("gpsimd", arena[:, D * k:D * (k + 1)],
                w_in["we"][128 * k:128 * (k + 1), :]) for k in range(KC)]
    t_stash = [dma("gpsimd", we_bf[:, D * k:D * (k + 1)], arena[:, D * k:D * (k + 1)],
                   [t_we[k]]) for k in range(KC)]

    tok_spT = [None] * NT
    for t in range(NT):
        p = t % 2
        tx = dma("sync", xin[p][:], x_in[128 * t:128 * (t + 1), :], [F(f"xin{p}")])
        tnorm = stats_norm(t, xin[p][:], LNB[1], tx, xf)
        free[f"xin{p}"] = tnorm
        tev = transpose_tile(t, tnorm)
        free[f"xf{p}"] = tev
        tsp = dma("sync", xfT_sp[0][:, D * t:D * (t + 1)], xfT[p][:], [tev])
        tok_spT[t] = tsp
        trk, tvs_last = router_softmax(t, tev, S1tm, cc_in[0])
        free[f"xfT{p}"] = [tsp, trk]
    tcs1 = sexp_batch(S1tm, cc_in[0], tvs_last)

    tok_zw = z_phase(xfT_sp[0], tok_spT, z1_sp, t_we)
    p1b_last_mm = ("ts", P.cnt["ts"], "tensor")
    free["wreg0"] = p1b_last_mm
    free["wreg1"] = p1b_last_mm

    tth1 = gather_bisect(0, tcs1)
    ths1 = s_values(S1tm, s1c, hs1, tth1)

    if DEV_TAPS:
        dma("sync", dbg[:, 0:4096], Sg[:], [ths1])
        dma("sync", dbg[:, 4096:4608], S1tm[:], [ths1])
        dma("sync", dbg[:, 4608:4640], s1c, [ths1])
        dma("sync", dbg[:, 4640:4672], hs1, [ths1])
        dma("sync", dbg[:, 4672:4688], thbc, [ths1])

    t_w8 = {}
    t_bb = {}

    def load_eighth_steps(e8):
        reg = (e8 % 2) * WREG
        tws = {}
        tbb = {}
        first = True
        n_emitted = 0
        for wi, w in enumerate(("wv", "wk", "wg", "wq")):
            tw = None
            for i in range(4):
                src = w_in[w][512 * i:512 * (i + 1), W8 * e8:W8 * (e8 + 1)] \
                    .rearrange("(c p) j -> p c j", p=128)
                dst = arena[:, reg + 4096 * wi + 1024 * i: reg + 4096 * wi + 1024 * (i + 1)] \
                    .rearrange("p (c j) -> p c j", j=W8)
                tw = dma("gpsimd", dst, src, [F(f"wreg{e8 % 2}")] if first else [])
                first = False
                n_emitted += 1
                if n_emitted % 2 == 0:
                    yield tws, tbb
            tws[w] = tw
        for w in ("bq", "bk", "bv", "bg"):
            tbb[w] = dma("gpsimd", attb[w][e8 % 2][:],
                         b_in[w][:, W8 * e8:W8 * (e8 + 1)].partition_broadcast(128),
                         [F(f"attb{e8 % 2}")])
        yield tws, tbb

    def load_eighth(e8):
        for tws, tbb in load_eighth_steps(e8):
            pass
        return tws, tbb

    t_w8[0], t_bb[0] = load_eighth(0)

    tok_x2w = [None] * NT
    tok_xf2T = [None] * NT
    for t in range(NT):
        p = t % 2
        tx = dma("sync", xin[p][:], x_in[128 * t:128 * (t + 1), :], [F(f"xin{p}")])
        tz = dma("sync", zr[p][:], z1_sp[128 * t:128 * (t + 1), :],
                 [tok_zw[t], F(f"zr{p}")])
        tv1 = P.ev("vector", lambda e, o=bigf[p][:], z=zr[p][:], h=hs1[:, t:t + 1], x=xin[p][:]:
                   e.scalar_tensor_tensor(out=o, in0=z, scalar=h, in1=x,
                                          op0=ALU.mult, op1=ALU.add),
                   [tx, tz, ths1, F(f"bigf{p}")], inc="vs")
        free[f"xin{p}"] = tv1
        free[f"zr{p}"] = tv1
        tv2 = P.ev("vector", lambda e, o=bigf[p][:], h=hs1[:, t:t + 1]:
                   e.scalar_tensor_tensor(out=o, in0=beb[:], scalar=h, in1=o,
                                          op0=ALU.mult, op1=ALU.add), [t_beb], inc="vs")
        tnorm = stats_norm(t, bigf[p][:], LNB[2], tv2, xf)
        tev = transpose_tile(t, tnorm)
        free[f"xf{p}"] = tev
        tsp = dma("sync", xfT_sp[1][:, D * t:D * (t + 1)], xfT[p][:], [tev])
        tok_xf2T[t] = tsp
        free[f"xfT{p}"] = tsp
        tcast = dma("gpsimd", x2_sp[128 * t:128 * (t + 1), :], bigf[p][:], [tv2])
        tok_x2w[t] = tcast
        free[f"bigf{p}"] = [tnorm, tcast]

    tok_x3w = [[None] * 8 for _ in range(NT)]
    WORD = ("wv", "wk", "wg", "wq")
    for e8 in range(8):
        reg = (e8 % 2) * WREG
        last_mm = None
        last_bias_read = None
        next_loader = load_eighth_steps(e8 + 1) if e8 + 1 < 8 else None
        for t in range(NT):
            if next_loader is not None and t >= 4:
                try:
                    t_w8[e8 + 1], t_bb[e8 + 1] = next(next_loader)
                except StopIteration:
                    next_loader = None
            p = t % 2
            j = t % SEQT
            tl = dma("sync", xfTr[p][:], xfT_sp[1][:, D * t:D * (t + 1)],
                     [tok_xf2T[t], F(f"xfTr{p}")])
            tx2 = dma("sync", x2q[p], x2_sp[128 * t:128 * (t + 1), W8 * e8:W8 * (e8 + 1)],
                      [tok_x2w[t], F(f"x2q{p}")])
            tmm = {}
            for wi, w in enumerate(WORD):
                for k in range(KC):
                    tmm[w] = P.ev("tensor",
                                  lambda e, l=xfTr[p][:, 128 * k:128 * (k + 1)],
                                  r=arena[:, reg + 4096 * wi + W8 * k: reg + 4096 * wi + W8 * (k + 1)],
                                  o=ps_z[wi][:, 0:W8], s=(k == 0), st=(k == KC - 1):
                                  e.matmul(o, l, r, start=s, stop=st),
                                  [tl, t_w8[e8][w], F(f"psz{wi}")] if k == 0 else [],
                                  inc="ts" if k == KC - 1 else None)
            free[f"xfTr{p}"] = tmm["wq"]
            tvv = P.ev("vector", lambda e, b=attb["bv"][e8 % 2][:]:
                       e.tensor_add(v_sb, ps_z[0][:, 0:W8], b),
                       [tmm["wv"], t_bb[e8]["bv"]], inc="vs")
            tvk = P.ev("vector", lambda e, b=attb["bk"][e8 % 2][:]:
                       e.tensor_add(k_sb, ps_z[1][:, 0:W8], b),
                       [tmm["wk"], t_bb[e8]["bk"]], inc="vs")
            tkv = P.ev("vector", lambda e: e.tensor_mul(kvb, k_sb, v_sb),
                       [F("kvb")], inc="vs")
            free["psz0"] = tkv
            free["psz1"] = tkv
            tc1 = P.ev("tensor", lambda e, st=(j == 0): e.matmul(
                ps_c[:, 0:W8], tri[:], kvb, start=True, stop=st),
                [tkv, t_tri, F("psc")], inc="ts" if j == 0 else None)
            if j > 0:
                tc1 = P.ev("tensor", lambda e: e.matmul(
                    ps_c[:, 0:W8], ones1[:], carryq[0:1, 0:W8], start=False, stop=True),
                    [t_one, F("carryq")], inc="ts")
            free["kvb"] = tc1
            free["carryq_w"] = tc1 if j > 0 else free.get("carryq_w")
            tvg = P.ev("vector", lambda e, b=attb["bg"][e8 % 2][:]:
                       e.tensor_add(g_sb, ps_z[2][:, 0:W8], b),
                       [tmm["wg"], t_bb[e8]["bg"], F("sig_w")], inc="vs")
            free["psz2"] = tvg
            tsg_ = P.ev("scalar", lambda e: e.activation(out=sig_sb, in_=g_sb,
                                                         func=ACT.Sigmoid),
                        [tvg, F("sig_r")], inc="ss")
            free["sig_w"] = tsg_
            tvq = P.ev("vector", lambda e, b=attb["bq"][e8 % 2][:]:
                       e.tensor_add(q_sb, ps_z[3][:, 0:W8], b),
                       [tmm["wq"], t_bb[e8]["bq"]], inc="vs")
            free["psz3"] = tvq
            last_bias_read = tvq
            if j < SEQT - 1:
                tcx = P.ev("vector", lambda e: e.tensor_copy(carrst[96:128, 0:W8],
                                                             ps_c[96:128, 0:W8]),
                           [tc1, F("carrst")], inc="vs")
                tcd = dma("gpsimd", carryq[0:1, 0:W8], carrst[127:128, 0:W8],
                          [tcx, F("carryq_w")])
                free["carrst"] = tcd
                free["carryq"] = tcd
            tv5 = P.ev("vector", lambda e: e.tensor_mul(t1_sb, q_sb, ps_c[:, 0:W8]),
                       [tc1], inc="vs")
            free["psc"] = tv5
            tv6 = P.ev("vector", lambda e: e.tensor_mul(t1_sb, t1_sb, sig_sb),
                       [tsg_], inc="vs")
            free["sig_r"] = tv6
            tv7 = P.ev("vector", lambda e, o=x3q[p], x2b=x2q[p]: e.tensor_add(o, t1_sb, x2b),
                       [tx2, F(f"x3q{p}")], inc="vs")
            free[f"x2q{p}"] = tv7
            col = 8 * t + e8
            P.ev("vector", lambda e, o=s3sum[:, col:col + 1], i=x3q[p]:
                 e.reduce_sum(o, i, axis=AX), [])
            ts8 = P.ev("scalar", lambda e, i=x3q[p], a=s3sq[:, col:col + 1]:
                       e.activation(out=junk_bf[:, 0:W8], in_=i, func=ACT.Square,
                                    accum_out=a), [tv7, F("junk")], inc="ss")
            free["junk"] = ts8
            tcast = dma("gpsimd", x3_sp[128 * t:128 * (t + 1), W8 * e8:W8 * (e8 + 1)],
                        x3q[p], [tv7])
            tok_x3w[t][e8] = tcast
            free[f"x3q{p}"] = [ts8, tcast]
            last_mm = tmm["wq"]
        assert next_loader is None or e8 + 1 >= 8
        free[f"wreg{e8 % 2}"] = last_mm
        free[f"attb{e8 % 2}"] = last_bias_read

    tok_xf3T = [None] * NT
    for t in range(NT):
        p = t % 2
        tx = dma("sync", x3rA[p][:], x3_sp[128 * t:128 * (t + 1), :],
                 [tok_x3w[t][7], F(f"x3rA{p}")])
        base = LNB[3]
        P.ev("vector", lambda e, o=scol(ST["sum"] + base, t), i=s3sum[:, 8 * t:8 * (t + 1)]:
             e.reduce_sum(o, i, axis=AX), [])
        tvB = P.ev("vector", lambda e, o=scol(ST["sq"] + base, t), i=s3sq[:, 8 * t:8 * (t + 1)]:
                   e.reduce_sum(o, i, axis=AX), [], inc="vs")
        vnop()
        tvm1 = P.ev("vector", lambda e, o=scol(ST["m"] + base, t), i=scol(ST["sum"] + base, t):
                    e.tensor_scalar_mul(o, i, 1.0 / D), [], inc="vs")
        tsq2 = P.ev("scalar", lambda e, o=scol(ST["var"] + base, t), i=scol(ST["m"] + base, t):
                    e.activation(out=o, in_=i, func=ACT.Square), [tvm1], inc="ss")
        tv2 = P.ev("vector", lambda e, o=scol(ST["var"] + base, t), i=scol(ST["sq"] + base, t):
                   e.scalar_tensor_tensor(out=o, in0=i, scalar=1.0 / D, in1=o,
                                          op0=ALU.mult, op1=ALU.subtract),
                   [tvB, tsq2], inc="vs")
        P.ev("scalar", lambda e, o=scol(ST["rstd"] + base, t), i=scol(ST["var"] + base, t):
             e.activation(out=o, in_=i, func=ACT.Ln, bias=c_eps), [tv2, t_cst])
        ts3 = P.ev("scalar", lambda e, o=scol(ST["rstd"] + base, t):
                   e.activation(out=o, in_=o, func=ACT.Exp, scale=c_nh), [], inc="ss")
        vnop(2)
        tv3 = P.ev("vector", lambda e, o=scol(ST["mb"] + base, t), m=scol(ST["m"] + base, t),
                   r=scol(ST["rstd"] + base, t):
                   e.scalar_tensor_tensor(out=o, in0=m, scalar=-1.0, in1=r,
                                          op0=ALU.mult, op1=ALU.mult), [ts3], inc="vs")
        tnorm = P.ev("scalar", lambda e, o=xf[p][:], i=x3rA[p][:], r=scol(ST["rstd"] + base, t),
                     b=scol(ST["mb"] + base, t):
                     e.activation(out=o, in_=i, func=ACT.Identity, scale=r, bias=b),
                     [tv3, tx, F(f"xf{p}")], inc="ss")
        free[f"x3rA{p}"] = tnorm
        tev = transpose_tile(t, tnorm)
        free[f"xf{p}"] = tev
        tsp = dma("sync", xfT_sp[2][:, D * t:D * (t + 1)], xfT[p][:], [tev])
        tok_xf3T[t] = tsp
        trk, tvs_last = router_softmax(t, tev, S2tm, cc_in[1])
        free[f"xfT{p}"] = [tsp, trk]

    t_we2 = []
    for k in range(KC):
        q = "sync" if k % 2 == 0 else "gpsimd"
        ws = [t_stash[k]]
        if k < 2:
            ws += [F("wreg0"), F("wreg1")]
        t_we2.append(dma(q, arena[:, D * k:D * (k + 1)],
                         we_bf[:, D * k:D * (k + 1)], ws))

    tcs2 = sexp_batch(S2tm, cc_in[1], tvs_last)
    tth2 = gather_bisect(1, tcs2)
    ths2 = s_values(S2tm, s2c, hs2, tth2)

    tok_z2w = z_phase(xfT_sp[2], tok_xf3T, z2_sp, t_we2)

    tok_y = [None] * NT
    for t in range(NT):
        p = t % 2
        tx = dma("sync", x3rB[p][:], x3_sp[128 * t:128 * (t + 1), :],
                 [tok_x3w[t][7], F(f"x3rB{p}")])
        tz = dma("sync", zr[p][:], z2_sp[128 * t:128 * (t + 1), :],
                 [tok_z2w[t], F(f"zr{p}")])
        tv1 = P.ev("vector", lambda e, o=bigf[p][:], z=zr[p][:], h=hs2[:, t:t + 1], x=x3rB[p][:]:
                   e.scalar_tensor_tensor(out=o, in0=z, scalar=h, in1=x,
                                          op0=ALU.mult, op1=ALU.add),
                   [tx, tz, ths2, F(f"bigf{p}")], inc="vs")
        free[f"zr{p}"] = tv1
        free[f"x3rB{p}"] = tv1
        tv2 = P.ev("vector", lambda e, o=bigf[p][:], h=hs2[:, t:t + 1]:
                   e.scalar_tensor_tensor(out=o, in0=beb[:], scalar=h, in1=o,
                                          op0=ALU.mult, op1=ALU.add), [t_beb], inc="vs")
        ty = dma("sync", y_out[128 * t:128 * (t + 1), :], bigf[p][:], [tv2])
        tok_y[t] = ty
        free[f"bigf{p}"] = ty

    P.ev("sync", None, [tok_y[NT - 1]])

    with nc.Block() as block:
        def emitter(engname):
            def emit(eng):
                for ename, ws, fn, incs in P.events:
                    if ename != engname:
                        continue
                    for s, v in ws:
                        eng.wait_ge(sems[s], v)
                    if fn is None:
                        continue
                    ins = fn(eng)
                    for s, n in incs:
                        ins = ins.then_inc(sems[s], n)
            return emit
        block.sync(emitter("sync"))
        block.gpsimd(emitter("gpsimd"))
        block.vector(emitter("vector"))
        block.scalar(emitter("scalar"))
        block.tensor(emitter("tensor"))

    ctx.close()
    return nc


def kernel(L0_attn_x, L0_attn_Wq, L0_attn_bq, L0_attn_Wk, L0_attn_bk,
           L0_attn_Wv, L0_attn_bv, L0_attn_Wg, L0_attn_bg,
           L0_ffn_router_w, L0_ffn_router_b, L0_ffn_expert_w, L0_ffn_expert_b):
    x = np.ascontiguousarray(np.asarray(L0_attn_x, dtype=np.float32)).reshape(8, 4096, 2048)
    f32 = lambda a: np.ascontiguousarray(np.asarray(a, dtype=np.float32))
    consts = {
        "wq": f32(np.asarray(L0_attn_Wq).T), "wk": f32(np.asarray(L0_attn_Wk).T),
        "wv": f32(np.asarray(L0_attn_Wv).T),
        "wg": f32(np.asarray(L0_attn_Wg).T), "we": f32(L0_ffn_expert_w),
        "rw": f32(L0_ffn_router_w),
        "bq": f32(L0_attn_bq).reshape(1, 2048), "bk": f32(L0_attn_bk).reshape(1, 2048),
        "bv": f32(L0_attn_bv).reshape(1, 2048), "bg": f32(L0_attn_bg).reshape(1, 2048),
        "be": f32(L0_ffn_expert_b).reshape(1, 2048),
        "rb": f32(L0_ffn_router_b).reshape(1, 16),
        "id_bf": np.eye(128, dtype=ml_dtypes.bfloat16),
        "id_f": np.eye(128, dtype=np.float32),
        "tri": np.triu(np.ones((128, 128))).astype(ml_dtypes.bfloat16),
        "ones1": np.ones((1, 128), dtype=np.float32),
        "gmat": (np.arange(128)[:, None] % 16 ==
                 np.arange(128)[None, :] % 16).astype(np.float32),
    }
    nc = build_nc()
    in_maps = [dict(consts, x=np.ascontiguousarray(x[i])) for i in range(8)]
    res = run_bass_kernel_spmd(nc, in_maps, core_ids=list(range(8)))
    y = np.concatenate([res.results[i]["y"].reshape(2, 2048, 2048) for i in range(8)], axis=0)
    return y


# version 13 (final)
# speedup vs baseline: 1.0913x; 1.0913x over previous
import numpy as np
import ml_dtypes
import concourse.bass as bass
import concourse.mybir as mybir
from concourse.bass_utils import run_bass_kernel_spmd
from contextlib import ExitStack

F32 = mybir.dt.float32
BF16 = mybir.dt.bfloat16
FP8 = mybir.dt.float8e4
AX = mybir.AxisListType.X
ALU = mybir.AluOpType
ACT = mybir.ActivationFunctionType

NCORES = 8
T_CORE = 4096
NT = 32
D = 2048
KC = 16
NE = 16
TOPK = 2048.0
BISECT_ITERS = 34
EPS = 1e-5
SEQT = 16
W8 = 256
WREG = 16384

DEV_TAPS = False


class Prog:

    DMA_SEMS = ("dx", "dg", "cc")

    def __init__(self):
        self.cnt = {"dx": 0, "dg": 0, "cc": 0, "ts": 0, "vs": 0, "ss": 0, "gs": 0}
        self.step = {"dx": 16, "dg": 16, "cc": 1, "ts": 1, "vs": 1, "ss": 1, "gs": 1}
        self.events = []
        self.floor = {}

    def ev(self, eng, fn, waits=(), inc=None):
        ws = []
        seen = {}
        for tk in _flat(waits):
            sem, val, peng = tk
            if peng == eng and sem not in self.DMA_SEMS:
                continue
            if seen.get(sem, -1) < val:
                seen[sem] = val
        for sem, val in seen.items():
            if self.floor.get((eng, sem), -1) >= val:
                continue
            self.floor[(eng, sem)] = val
            ws.append((sem, val))
        incs = []
        tok = None
        if inc is not None:
            self.cnt[inc] += self.step[inc]
            incs.append((inc, self.step[inc]))
            tok = (inc, self.cnt[inc], eng)
            if inc not in self.DMA_SEMS:
                self.floor[(eng, inc)] = self.cnt[inc]
        self.events.append((eng, ws, fn, incs))
        return tok


def _flat(x):
    if x is None:
        return
    if isinstance(x, tuple) and len(x) == 3 and isinstance(x[0], str):
        yield x
        return
    for it in x:
        yield from _flat(it)


def build_nc():
    nc = bass.Bass("TRN2", target_bir_lowering=False, debug=False)
    io = "ExternalOutput"

    x_in = nc.dram_tensor("x", [T_CORE, D], F32, kind="ExternalInput").ap()
    w_in = {n: nc.dram_tensor(n, [D, D], F32, kind="ExternalInput").ap()
            for n in ("wq", "wk", "wv", "wg", "we")}
    rw_in = nc.dram_tensor("rw", [D, NE], F32, kind="ExternalInput").ap()
    b_in = {n: nc.dram_tensor(n, [1, D], F32, kind="ExternalInput").ap()
            for n in ("bq", "bk", "bv", "bg", "be")}
    rb_in = nc.dram_tensor("rb", [1, NE], F32, kind="ExternalInput").ap()
    id_bf_in = nc.dram_tensor("id_bf", [128, 128], BF16, kind="ExternalInput").ap()
    id_f_in = nc.dram_tensor("id_f", [128, 128], F32, kind="ExternalInput").ap()
    tri_in = nc.dram_tensor("tri", [128, 128], BF16, kind="ExternalInput").ap()
    ones_in = nc.dram_tensor("ones1", [1, 128], F32, kind="ExternalInput").ap()
    gmat_in = nc.dram_tensor("gmat", [128, 128], F32, kind="ExternalInput").ap()
    we_bf = nc.dram_tensor("we_bf", [128, KC * D], BF16, kind="ExternalOutput").ap()
    y_out = nc.dram_tensor("y", [T_CORE, D], F32, kind="ExternalOutput").ap()

    xfT_sp = [nc.dram_tensor(f"xf{i}T_sp", [128, NT * D], BF16, kind=io).ap()
              for i in (1, 2, 3)]
    z1_sp = nc.dram_tensor("z1_sp", [T_CORE, D], BF16, kind=io).ap()
    x2_sp = nc.dram_tensor("x2_sp", [T_CORE, D], BF16, kind=io).ap()
    x3_sp = nc.dram_tensor("x3_sp", [T_CORE, D], BF16, kind=io).ap()
    z2_sp = nc.dram_tensor("z2_sp", [T_CORE, D], BF16, kind=io).ap()
    cc_in = [nc.dram_tensor(f"cc{i}_in", [NE, T_CORE], F32).ap() for i in (1, 2)]
    cc_out = [nc.dram_tensor(f"cc{i}_out", [128, T_CORE], F32).ap() for i in (1, 2)]
    cnt_d = nc.dram_tensor("cnt_d", [16, 8], F32).ap()
    dbg = (nc.dram_tensor("dbg", [128, 4096 + 512 + 32 + 32 + 16], F32,
                          kind="ExternalOutput").ap() if DEV_TAPS else None)
    th_d = nc.dram_tensor("th_d", [16, 1], F32).ap()

    ctx = ExitStack()
    sb = lambda name, shape, dt: ctx.enter_context(nc.sbuf_tensor(name, shape, dt))

    arena = sb("arena", [128, 2 * WREG], BF16)
    xin = [sb(f"xin{i}", [128, D], F32) for i in (0, 1)]
    bigf = [sb(f"bigf{i}", [128, D], F32) for i in (0, 1)]
    x3rA = [sb(f"x3rA{i}", [128, D], BF16) for i in (0, 1)]
    x3rB = [sb(f"x3rB{i}", [128, D], BF16) for i in (0, 1)]
    zr = [sb(f"zr{i}", [128, D], BF16) for i in (0, 1)]
    xf = [sb(f"xf{i}", [128, D], BF16) for i in (0, 1)]
    xfT = [sb(f"xfT{i}", [128, D], BF16) for i in (0, 1)]
    xfTr = [sb(f"xfTr{i}", [128, D], BF16) for i in (0, 1)]
    zev = [sb(f"zev{i}", [128, 512], BF16) for i in range(4)]
    junk = sb("junk", [128, 4096], FP8)
    junk_bf = junk.bitcast(BF16)[:, 0:D]
    Sg = sb("Sg", [128, T_CORE], F32)
    S1tm = sb("S1tm", [128, NT * NE], F32)
    S2tm = sb("S2tm", [128, NT * NE], F32)
    sxev = [sb(f"sxev{i}", [16, 128], F32) for i in (0, 1)]
    grp = sb("grp", [128, 4096], F32)
    beb = sb("beb", [128, D], BF16)
    small = sb("small", [128, 2048], F32)
    id_bf = sb("id_bf_sb", [128, 128], BF16)
    id_f = sb("id_f_sb", [128, 128], F32)
    tri = sb("tri_sb", [128, 128], BF16)
    ones1 = sb("ones1_sb", [1, 128], F32)
    gmat = sb("gmat_sb", [128, 128], F32)

    rw_sb = sb("rw_sb", [128, KC * NE], BF16)
    rbb = sb("rbb", [128, NE], F32)

    gb = grp.bitcast(BF16)
    x2q = [gb[:, 0:256], gb[:, 256:512]]
    kvb = gb[:, 512:768]
    attb = {w: [gb[:, 768 + (2 * wi + d) * 256: 768 + (2 * wi + d + 1) * 256] for d in (0, 1)]
            for wi, w in enumerate(("bq", "bk", "bv", "bg"))}
    x3q = [grp[:, 1408:1664], grp[:, 1664:1920]]
    v_sb = grp[:, 1920:2176]
    k_sb = grp[:, 2176:2432]
    g_sb = grp[:, 2432:2688]
    q_sb = grp[:, 2688:2944]
    sig_sb = grp[:, 2944:3200]
    t1_sb = grp[:, 3200:3456]
    carrst = grp[:, 3456:3712]
    carryq = grp[:, 3712:3968]
    b_lo = grp[:, 3968:3969]
    b_hi = grp[:, 3969:3970]
    b_mid = grp[:, 3970:3971]
    b_cntp = grp[:, 3971:3972]
    b_cntr = grp[:, 3972:3973]
    b_sel = grp.bitcast(mybir.dt.int32)[:, 3973:3974]
    b_cnt8 = grp[:, 3976:3984]
    thbc = grp[:, 3984:4000]

    def scol(off, t=0):
        return small[:, off + t: off + t + 1]
    ST = {"sum": 0, "sq": 32, "m": 64, "var": 96, "rstd": 128, "mb": 160}
    LNB = {1: 0, 2: 192, 3: 384}
    s3sum = small[:, 576:832]
    s3sq = small[:, 832:1088]
    s1c = small[:, 1088:1120]
    hs1 = small[:, 1120:1152]
    s2c = small[:, 1152:1184]
    hs2 = small[:, 1184:1216]
    MX, MXN, DEN, RDEN = 1216, 1220, 1224, 1228
    c_eps = small[:, 1232:1233]
    c_nh = small[:, 1233:1234]
    lgf = [small[:, 1344:1360], small[:, 1360:1376]]
    Ebuf = [small[:, 1376:1392], small[:, 1392:1408]]
    small_mask = small[:, 1472:1984]

    ps_z = [ctx.enter_context(nc.psum_tensor(f"psz{i}", [128, 512], F32)) for i in range(4)]
    ps_c = ctx.enter_context(nc.psum_tensor("psc", [128, 512], F32))
    ps_tr = [ctx.enter_context(nc.psum_tensor(f"pst{i}", [128, 1024], BF16)) for i in (0, 1)]
    ps_ls = ctx.enter_context(nc.psum_tensor("psls", [128, 512], F32))
    ps_lg = ps_ls[:, 0:16]
    ps_cnt = ps_ls[:, 256:257]
    ps_sx = ps_ls[0:16, 128:256]

    sems = {k: ctx.enter_context(nc.semaphore(k))
            for k in ("dx", "dg", "cc", "ts", "vs", "ss", "gs")}
    P = Prog()
    free = {}
    F = free.get

    def vnop(n=3):
        for _ in range(n):
            P.ev("vector", lambda e: e.engine_nop(), [])

    def dma(queue, dst, src, waits=(), slow=False):
        sem = "dg" if queue == "gpsimd" else "dx"
        return P.ev(queue, (lambda e, d=dst, s=src, sl=slow:
                            e.dma_start(out=d, in_=s, allow_slow_non_contiguous=sl)),
                    waits, inc=sem)

    t_cst = None
    ps_ls_chain = [None]

    def stats_norm(t, xsrc, base, src_tok, out_xf_t):
        p = t % 2
        tv = P.ev("vector", lambda e, o=scol(ST["sum"] + base, t), i=xsrc:
                  e.reduce_sum(o, i, axis=AX), [src_tok], inc="vs")
        ts_ = P.ev("scalar", lambda e, i=xsrc, a=scol(ST["sq"] + base, t):
                   e.activation(out=junk_bf, in_=i, func=ACT.Square, accum_out=a),
                   [src_tok, F("junk")], inc="ss")
        vnop()
        tvm1 = P.ev("vector", lambda e, o=scol(ST["m"] + base, t), i=scol(ST["sum"] + base, t):
                    e.tensor_scalar_mul(o, i, 1.0 / D), [tv], inc="vs")
        tsq2 = P.ev("scalar", lambda e, o=scol(ST["var"] + base, t), i=scol(ST["m"] + base, t):
                    e.activation(out=o, in_=i, func=ACT.Square), [tvm1], inc="ss")
        tv2 = P.ev("vector", lambda e, o=scol(ST["var"] + base, t), i=scol(ST["sq"] + base, t):
                   e.scalar_tensor_tensor(out=o, in0=i, scalar=1.0 / D, in1=o,
                                          op0=ALU.mult, op1=ALU.subtract),
                   [ts_, tsq2], inc="vs")
        P.ev("scalar", lambda e, o=scol(ST["rstd"] + base, t), i=scol(ST["var"] + base, t):
             e.activation(out=o, in_=i, func=ACT.Ln, bias=c_eps), [tv2, t_cst])
        ts3 = P.ev("scalar", lambda e, o=scol(ST["rstd"] + base, t):
                   e.activation(out=o, in_=o, func=ACT.Exp, scale=c_nh), [], inc="ss")
        vnop(2)
        tv3 = P.ev("vector", lambda e, o=scol(ST["mb"] + base, t), m=scol(ST["m"] + base, t),
                   r=scol(ST["rstd"] + base, t):
                   e.scalar_tensor_tensor(out=o, in0=m, scalar=-1.0, in1=r,
                                          op0=ALU.mult, op1=ALU.mult), [ts3], inc="vs")
        tnorm = P.ev("scalar", lambda e, o=xf[p][:], i=xsrc, r=scol(ST["rstd"] + base, t),
                     b=scol(ST["mb"] + base, t):
                     e.activation(out=o, in_=i, func=ACT.Identity, scale=r, bias=b),
                     [tv3, F(f"xf{p}")], inc="ss")
        free["junk"] = ts_
        return tnorm

    def transpose_tile(t, tnorm):
        p = t % 2
        tev = None
        for g in range(4):
            tt = None
            for b4 in range(4):
                b = 4 * g + b4
                tt = P.ev("tensor",
                          lambda e, o=ps_tr[g % 2][:, 128 * b4:128 * (b4 + 1)],
                          i=xf[p][:, 128 * b:128 * (b + 1)]:
                          e.transpose(o, i, id_bf[:]),
                          [tnorm, t_idb, F(f"pstr{g % 2}")],
                          inc="ts" if b4 == 3 else None)
            tev = P.ev("vector", lambda e, o=xfT[p][:, 512 * g:512 * (g + 1)],
                       i=ps_tr[g % 2][:, 0:512]: e.tensor_copy(o, i),
                       [tt, F(f"xfT{p}") if g == 0 else None], inc="vs")
            free[f"pstr{g % 2}"] = tev
        return tev

    def router_softmax(t, tev, Stm, cc_ap):
        p = t % 2
        tr = None
        for k in range(KC):
            tr = P.ev("tensor", lambda e, l=xfT[p][:, 128 * k:128 * (k + 1)],
                      r=rw_sb[:, NE * k:NE * (k + 1)], s=(k == 0), st=(k == KC - 1):
                      e.matmul(ps_lg, l, r, start=s, stop=st),
                      [tev, t_rw, ps_ls_chain[0]] if k == 0 else [],
                      inc="ts" if k == KC - 1 else None)
        tva = P.ev("vector", lambda e, o=lgf[p]: e.tensor_add(o, ps_lg, rbb[:]),
                   [tr, t_rbb, F(f"lgf{p}")], inc="vs")
        vnop()
        P.ev("vector", lambda e, o=scol(MX, p), i=lgf[p]: e.reduce_max(o, i, axis=AX), [])
        vnop()
        tvm = P.ev("vector", lambda e, o=scol(MXN, p), i=scol(MX, p):
                   e.tensor_scalar_mul(o, i, -1.0), [], inc="vs")
        tse = P.ev("scalar", lambda e, o=Ebuf[p], i=lgf[p], b=scol(MXN, p):
                   e.activation(out=o, in_=i, func=ACT.Exp, bias=b),
                   [tvm, F(f"Ebuf{p}")], inc="ss")
        P.ev("vector", lambda e, o=scol(DEN, p), i=Ebuf[p]: e.reduce_sum(o, i, axis=AX), [tse])
        vnop()
        tvr = P.ev("vector", lambda e, o=scol(RDEN, p), i=scol(DEN, p): e.reciprocal(o, i),
                   [], inc="vs")
        tvs = P.ev("scalar", lambda e, o=Stm[:, NE * t:NE * (t + 1)], i=Ebuf[p], r=scol(RDEN, p):
                   e.activation(out=o, in_=i, func=ACT.Identity, scale=r, bias=0.0),
                   [tvr], inc="ss")
        free[f"lgf{p}"] = tse
        free[f"Ebuf{p}"] = tvs
        ps_ls_chain[0] = tva
        return tr, tvs

    def sexp_batch(Stm, cc_ap, t_last_stm):
        tcs = None
        for t in range(NT):
            p = t % 2
            tts = P.ev("tensor", lambda e, i=Stm[:, NE * t:NE * (t + 1)]:
                       e.transpose(ps_sx, i, id_f[:]),
                       [t_last_stm, t_idf, ps_ls_chain[0]], inc="ts")
            tvx = P.ev("vector", lambda e, o=sxev[p][:]: e.tensor_copy(o, ps_sx),
                       [tts, F(f"sxev{p}")], inc="vs")
            ps_ls_chain[0] = tvx
            tcs = dma("sync", cc_ap[0:16, 128 * t:128 * (t + 1)], sxev[p][:], [tvx])
            free[f"sxev{p}"] = tcs
        return tcs

    def z_phase(sp_src, tok_src, z_dst, we_toks, injector=None):
        toks = [None] * NT
        for t in range(NT):
            p = t % 2
            tl = dma("sync", xfTr[p][:], sp_src[:, D * t:D * (t + 1)],
                     [tok_src[t], F(f"xfTr{p}")])
            tmm = [None] * 4
            for n in range(4):
                for k in range(KC):
                    tmm[n] = P.ev("tensor",
                                  lambda e, l=xfTr[p][:, 128 * k:128 * (k + 1)],
                                  r=arena[:, D * k + 512 * n: D * k + 512 * (n + 1)],
                                  o=ps_z[n][:], s=(k == 0), st=(k == KC - 1):
                                  e.matmul(o, l, r, start=s, stop=st),
                                  [tl, we_toks[k], F(f"psz{n}") if k == 0 else None],
                                  inc="ts" if k == KC - 1 else None)
                if injector is not None and n % 2 == 1 and t >= 2:
                    injector()
            free[f"xfTr{p}"] = tmm[3]
            tws = None
            for n in range(4):
                tse = P.ev("scalar", lambda e, o=zev[n][:], i=ps_z[n][:]:
                           e.activation(out=o, in_=i, func=ACT.Copy),
                           [tmm[n], F(f"zev{n}")], inc="ss")
                free[f"psz{n}"] = tse
                tw = dma("sync", z_dst[128 * t:128 * (t + 1), 512 * n:512 * (n + 1)],
                         zev[n][:], [tse])
                free[f"zev{n}"] = tw
                tws = tw
            toks[t] = tws
        return toks

    class Bisect:

        def __init__(self, idx, cc_wait):
            tcc = P.ev("gpsimd", lambda e, i=cc_in[idx], o=cc_out[idx]:
                       e.collective_compute("AllGather", ALU.bypass,
                                            replica_groups=[list(range(NCORES))],
                                            ins=[i.opt()], outs=[o.opt()]),
                       [cc_wait], inc="cc")
            tsg = dma("gpsimd", Sg[:], cc_out[idx], [tcc, F("bis")])
            P.ev("vector", lambda e: e.memset(b_lo, 0.0), [tsg])
            self.t_prev = P.ev("vector", lambda e: e.memset(b_hi, 1.0), [], inc="vs")
            self.i = 0

        def emit_iter(self):
            if self.i >= BISECT_ITERS:
                return
            self.i += 1
            P.ev("gpsimd", lambda e: e.tensor_add(b_mid, b_lo, b_hi), [self.t_prev])
            tgm = P.ev("gpsimd", lambda e: e.tensor_scalar_mul(b_mid, b_mid, 0.5),
                       [], inc="gs")
            tvc = P.ev("vector", lambda e: e.tensor_scalar(
                out=junk[:], in0=Sg[:], scalar1=b_mid, scalar2=0.0,
                op0=ALU.is_ge, op1=ALU.add, accum_out=b_cntp),
                [F("junk"), tgm], inc="vs")
            free["junk"] = tvc
            tmm_g = P.ev("tensor", lambda e: e.matmul(ps_cnt, gmat[:], b_cntp,
                                                      start=True, stop=True),
                         [tvc, t_gm, ps_ls_chain[0]], inc="ts")
            ps_ls_chain[0] = tmm_g
            P.ev("vector", lambda e: e.tensor_single_scalar(b_sel, ps_cnt, TOPK,
                                                            ALU.is_ge), [tmm_g])
            vnop(2)
            P.ev("vector", lambda e: e.select(b_lo, b_sel, b_mid, b_lo), [])
            self.t_prev = P.ev("vector", lambda e: e.select(b_hi, b_sel, b_hi, b_mid),
                               [], inc="vs")

        def finish(self):
            while self.i < BISECT_ITERS:
                self.emit_iter()
            td3 = dma("gpsimd", th_d, b_lo[0:16, 0:1], [self.t_prev])
            td4 = dma("gpsimd", thbc.rearrange("p e -> p e"),
                      th_d.rearrange("e one -> (e one)").unsqueeze(0)
                      .broadcast_to([128, 16]),
                      [td3, F("thbc")])
            free["bis"] = td4
            return td4

    def s_values(Stm, sc, hc, tth):
        thb = thbc.unsqueeze(1).broadcast_to([128, NT, NE])
        tk1 = P.ev("vector", lambda e, i=Stm[:]:
                   e.tensor_tensor(small_mask.rearrange("p (a b) -> p a b", b=NE),
                                   i.rearrange("p (a b) -> p a b", b=NE), thb, ALU.is_ge),
                   [tth], inc="vs")
        vnop()
        P.ev("vector", lambda e, i=Stm[:]: e.tensor_mul(small_mask, small_mask, i), [])
        vnop()
        tlast = None
        for t in range(NT):
            tlast = P.ev("vector", lambda e, o=sc[:, t:t + 1], i=small_mask[:, NE * t:NE * (t + 1)]:
                         e.reduce_sum(o, i, axis=AX), [], inc="vs")
        th = P.ev("gpsimd", lambda e: e.tensor_scalar_mul(hc, sc, 0.5), [tlast], inc="gs")
        free["thbc"] = tk1
        return th

    P.ev("vector", lambda e: e.memset(c_eps, EPS), [])
    t_cst = P.ev("vector", lambda e: e.memset(c_nh, -0.5), [], inc="vs")

    t_idb = dma("sync", id_bf[:], id_bf_in)
    t_idf = dma("sync", id_f[:], id_f_in)
    t_tri = dma("sync", tri[:], tri_in)
    t_one = dma("sync", ones1[:], ones_in)
    t_gm = dma("sync", gmat[:], gmat_in)

    t_rbb = dma("sync", rbb[:], rb_in.partition_broadcast(128))
    t_rw = None
    for i in range(4):
        src = rw_in[512 * i:512 * (i + 1), :].rearrange("(c p) e -> p c e", p=128)
        dst = rw_sb[:, 64 * i:64 * (i + 1)].rearrange("p (c e) -> p c e", e=NE)
        t_rw = dma("gpsimd", dst, src)
    t_beb = dma("gpsimd", beb[:], b_in["be"].partition_broadcast(128))
    t_we = [dma("gpsimd", arena[:, D * k:D * (k + 1)],
                w_in["we"][128 * k:128 * (k + 1), :]) for k in range(KC)]
    t_stash = [dma("gpsimd", we_bf[:, D * k:D * (k + 1)], arena[:, D * k:D * (k + 1)],
                   [t_we[k]]) for k in range(KC)]

    tok_spT = [None] * NT
    for t in range(NT):
        p = t % 2
        tx = dma("sync", xin[p][:], x_in[128 * t:128 * (t + 1), :], [F(f"xin{p}")])
        tnorm = stats_norm(t, xin[p][:], LNB[1], tx, xf)
        free[f"xin{p}"] = tnorm
        tev = transpose_tile(t, tnorm)
        free[f"xf{p}"] = tev
        tsp = dma("sync", xfT_sp[0][:, D * t:D * (t + 1)], xfT[p][:], [tev])
        tok_spT[t] = tsp
        trk, tvs_last = router_softmax(t, tev, S1tm, cc_in[0])
        free[f"xfT{p}"] = [tsp, trk]
    tcs1 = sexp_batch(S1tm, cc_in[0], tvs_last)

    bis1 = Bisect(0, tcs1)
    tok_zw = z_phase(xfT_sp[0], tok_spT, z1_sp, t_we, injector=bis1.emit_iter)
    p1b_last_mm = ("ts", P.cnt["ts"], "tensor")
    free["wreg0"] = p1b_last_mm
    free["wreg1"] = p1b_last_mm
    tth1 = bis1.finish()
    ths1 = s_values(S1tm, s1c, hs1, tth1)

    if DEV_TAPS:
        dma("sync", dbg[:, 0:4096], Sg[:], [ths1])
        dma("sync", dbg[:, 4096:4608], S1tm[:], [ths1])
        dma("sync", dbg[:, 4608:4640], s1c, [ths1])
        dma("sync", dbg[:, 4640:4672], hs1, [ths1])
        dma("sync", dbg[:, 4672:4688], thbc, [ths1])

    t_w8 = {}
    t_bb = {}

    def load_eighth_steps(e8):
        reg = (e8 % 2) * WREG
        tws = {}
        tbb = {}
        first = True
        n_emitted = 0
        for wi, w in enumerate(("wv", "wk", "wg", "wq")):
            tw = None
            for i in range(4):
                src = w_in[w][512 * i:512 * (i + 1), W8 * e8:W8 * (e8 + 1)] \
                    .rearrange("(c p) j -> p c j", p=128)
                dst = arena[:, reg + 4096 * wi + 1024 * i: reg + 4096 * wi + 1024 * (i + 1)] \
                    .rearrange("p (c j) -> p c j", j=W8)
                tw = dma("gpsimd", dst, src, [F(f"wreg{e8 % 2}")] if first else [])
                first = False
                n_emitted += 1
                if n_emitted % 2 == 0:
                    yield tws, tbb
            tws[w] = tw
        for w in ("bq", "bk", "bv", "bg"):
            tbb[w] = dma("gpsimd", attb[w][e8 % 2][:],
                         b_in[w][:, W8 * e8:W8 * (e8 + 1)].partition_broadcast(128),
                         [F(f"attb{e8 % 2}")])
        yield tws, tbb

    def load_eighth(e8):
        for tws, tbb in load_eighth_steps(e8):
            pass
        return tws, tbb

    t_w8[0], t_bb[0] = load_eighth(0)

    tok_x2w = [None] * NT
    tok_xf2T = [None] * NT
    for t in range(NT):
        p = t % 2
        tx = dma("sync", xin[p][:], x_in[128 * t:128 * (t + 1), :], [F(f"xin{p}")])
        tz = dma("sync", zr[p][:], z1_sp[128 * t:128 * (t + 1), :],
                 [tok_zw[t], F(f"zr{p}")])
        tv1 = P.ev("vector", lambda e, o=bigf[p][:], z=zr[p][:], h=hs1[:, t:t + 1], x=xin[p][:]:
                   e.scalar_tensor_tensor(out=o, in0=z, scalar=h, in1=x,
                                          op0=ALU.mult, op1=ALU.add),
                   [tx, tz, ths1, F(f"bigf{p}")], inc="vs")
        free[f"xin{p}"] = tv1
        free[f"zr{p}"] = tv1
        tv2 = P.ev("vector", lambda e, o=bigf[p][:], h=hs1[:, t:t + 1]:
                   e.scalar_tensor_tensor(out=o, in0=beb[:], scalar=h, in1=o,
                                          op0=ALU.mult, op1=ALU.add), [t_beb], inc="vs")
        tnorm = stats_norm(t, bigf[p][:], LNB[2], tv2, xf)
        tev = transpose_tile(t, tnorm)
        free[f"xf{p}"] = tev
        tsp = dma("sync", xfT_sp[1][:, D * t:D * (t + 1)], xfT[p][:], [tev])
        tok_xf2T[t] = tsp
        free[f"xfT{p}"] = tsp
        tcast = dma("gpsimd", x2_sp[128 * t:128 * (t + 1), :], bigf[p][:], [tv2])
        tok_x2w[t] = tcast
        free[f"bigf{p}"] = [tnorm, tcast]

    tok_x3w = [[None] * 8 for _ in range(NT)]
    WORD = ("wv", "wk", "wg", "wq")
    for e8 in range(8):
        reg = (e8 % 2) * WREG
        last_mm = None
        last_bias_read = None
        next_loader = load_eighth_steps(e8 + 1) if e8 + 1 < 8 else None
        for t in range(NT):
            if next_loader is not None and t >= 4:
                try:
                    t_w8[e8 + 1], t_bb[e8 + 1] = next(next_loader)
                except StopIteration:
                    next_loader = None
            p = t % 2
            j = t % SEQT
            tl = dma("sync", xfTr[p][:], xfT_sp[1][:, D * t:D * (t + 1)],
                     [tok_xf2T[t], F(f"xfTr{p}")])
            tx2 = dma("sync", x2q[p], x2_sp[128 * t:128 * (t + 1), W8 * e8:W8 * (e8 + 1)],
                      [tok_x2w[t], F(f"x2q{p}")])
            tmm = {}
            for wi, w in enumerate(WORD):
                for k in range(KC):
                    tmm[w] = P.ev("tensor",
                                  lambda e, l=xfTr[p][:, 128 * k:128 * (k + 1)],
                                  r=arena[:, reg + 4096 * wi + W8 * k: reg + 4096 * wi + W8 * (k + 1)],
                                  o=ps_z[wi][:, 0:W8], s=(k == 0), st=(k == KC - 1):
                                  e.matmul(o, l, r, start=s, stop=st),
                                  [tl, t_w8[e8][w], F(f"psz{wi}")] if k == 0 else [],
                                  inc="ts" if k == KC - 1 else None)
            free[f"xfTr{p}"] = tmm["wq"]
            tvv = P.ev("vector", lambda e, b=attb["bv"][e8 % 2][:]:
                       e.tensor_add(v_sb, ps_z[0][:, 0:W8], b),
                       [tmm["wv"], t_bb[e8]["bv"]], inc="vs")
            tvk = P.ev("vector", lambda e, b=attb["bk"][e8 % 2][:]:
                       e.tensor_add(k_sb, ps_z[1][:, 0:W8], b),
                       [tmm["wk"], t_bb[e8]["bk"]], inc="vs")
            tkv = P.ev("vector", lambda e: e.tensor_mul(kvb, k_sb, v_sb),
                       [F("kvb")], inc="vs")
            free["psz0"] = tkv
            free["psz1"] = tkv
            tc1 = P.ev("tensor", lambda e, st=(j == 0): e.matmul(
                ps_c[:, 0:W8], tri[:], kvb, start=True, stop=st),
                [tkv, t_tri, F("psc")], inc="ts" if j == 0 else None)
            if j > 0:
                tc1 = P.ev("tensor", lambda e: e.matmul(
                    ps_c[:, 0:W8], ones1[:], carryq[0:1, 0:W8], start=False, stop=True),
                    [t_one, F("carryq")], inc="ts")
            free["kvb"] = tc1
            free["carryq_w"] = tc1 if j > 0 else free.get("carryq_w")
            tvg = P.ev("vector", lambda e, b=attb["bg"][e8 % 2][:]:
                       e.tensor_add(g_sb, ps_z[2][:, 0:W8], b),
                       [tmm["wg"], t_bb[e8]["bg"], F("sig_w")], inc="vs")
            free["psz2"] = tvg
            tsg_ = P.ev("scalar", lambda e: e.activation(out=sig_sb, in_=g_sb,
                                                         func=ACT.Sigmoid),
                        [tvg, F("sig_r")], inc="ss")
            free["sig_w"] = tsg_
            tvq = P.ev("vector", lambda e, b=attb["bq"][e8 % 2][:]:
                       e.tensor_add(q_sb, ps_z[3][:, 0:W8], b),
                       [tmm["wq"], t_bb[e8]["bq"]], inc="vs")
            free["psz3"] = tvq
            last_bias_read = tvq
            if j < SEQT - 1:
                tcx = P.ev("vector", lambda e: e.tensor_copy(carrst[96:128, 0:W8],
                                                             ps_c[96:128, 0:W8]),
                           [tc1, F("carrst")], inc="vs")
                tcd = dma("gpsimd", carryq[0:1, 0:W8], carrst[127:128, 0:W8],
                          [tcx, F("carryq_w")])
                free["carrst"] = tcd
                free["carryq"] = tcd
            tv5 = P.ev("vector", lambda e: e.tensor_mul(t1_sb, q_sb, ps_c[:, 0:W8]),
                       [tc1], inc="vs")
            free["psc"] = tv5
            tv6 = P.ev("vector", lambda e: e.tensor_mul(t1_sb, t1_sb, sig_sb),
                       [tsg_], inc="vs")
            free["sig_r"] = tv6
            tv7 = P.ev("vector", lambda e, o=x3q[p], x2b=x2q[p]: e.tensor_add(o, t1_sb, x2b),
                       [tx2, F(f"x3q{p}")], inc="vs")
            free[f"x2q{p}"] = tv7
            col = 8 * t + e8
            P.ev("vector", lambda e, o=s3sum[:, col:col + 1], i=x3q[p]:
                 e.reduce_sum(o, i, axis=AX), [])
            ts8 = P.ev("scalar", lambda e, i=x3q[p], a=s3sq[:, col:col + 1]:
                       e.activation(out=junk_bf[:, 0:W8], in_=i, func=ACT.Square,
                                    accum_out=a), [tv7, F("junk")], inc="ss")
            free["junk"] = ts8
            tcast = dma("gpsimd", x3_sp[128 * t:128 * (t + 1), W8 * e8:W8 * (e8 + 1)],
                        x3q[p], [tv7])
            tok_x3w[t][e8] = tcast
            free[f"x3q{p}"] = [ts8, tcast]
            last_mm = tmm["wq"]
        assert next_loader is None or e8 + 1 >= 8
        free[f"wreg{e8 % 2}"] = last_mm
        free[f"attb{e8 % 2}"] = last_bias_read

    tok_xf3T = [None] * NT
    for t in range(NT):
        p = t % 2
        tx = dma("sync", x3rA[p][:], x3_sp[128 * t:128 * (t + 1), :],
                 [tok_x3w[t][7], F(f"x3rA{p}")])
        base = LNB[3]
        P.ev("vector", lambda e, o=scol(ST["sum"] + base, t), i=s3sum[:, 8 * t:8 * (t + 1)]:
             e.reduce_sum(o, i, axis=AX), [])
        tvB = P.ev("vector", lambda e, o=scol(ST["sq"] + base, t), i=s3sq[:, 8 * t:8 * (t + 1)]:
                   e.reduce_sum(o, i, axis=AX), [], inc="vs")
        vnop()
        tvm1 = P.ev("vector", lambda e, o=scol(ST["m"] + base, t), i=scol(ST["sum"] + base, t):
                    e.tensor_scalar_mul(o, i, 1.0 / D), [], inc="vs")
        tsq2 = P.ev("scalar", lambda e, o=scol(ST["var"] + base, t), i=scol(ST["m"] + base, t):
                    e.activation(out=o, in_=i, func=ACT.Square), [tvm1], inc="ss")
        tv2 = P.ev("vector", lambda e, o=scol(ST["var"] + base, t), i=scol(ST["sq"] + base, t):
                   e.scalar_tensor_tensor(out=o, in0=i, scalar=1.0 / D, in1=o,
                                          op0=ALU.mult, op1=ALU.subtract),
                   [tvB, tsq2], inc="vs")
        P.ev("scalar", lambda e, o=scol(ST["rstd"] + base, t), i=scol(ST["var"] + base, t):
             e.activation(out=o, in_=i, func=ACT.Ln, bias=c_eps), [tv2, t_cst])
        ts3 = P.ev("scalar", lambda e, o=scol(ST["rstd"] + base, t):
                   e.activation(out=o, in_=o, func=ACT.Exp, scale=c_nh), [], inc="ss")
        vnop(2)
        tv3 = P.ev("vector", lambda e, o=scol(ST["mb"] + base, t), m=scol(ST["m"] + base, t),
                   r=scol(ST["rstd"] + base, t):
                   e.scalar_tensor_tensor(out=o, in0=m, scalar=-1.0, in1=r,
                                          op0=ALU.mult, op1=ALU.mult), [ts3], inc="vs")
        tnorm = P.ev("scalar", lambda e, o=xf[p][:], i=x3rA[p][:], r=scol(ST["rstd"] + base, t),
                     b=scol(ST["mb"] + base, t):
                     e.activation(out=o, in_=i, func=ACT.Identity, scale=r, bias=b),
                     [tv3, tx, F(f"xf{p}")], inc="ss")
        free[f"x3rA{p}"] = tnorm
        tev = transpose_tile(t, tnorm)
        free[f"xf{p}"] = tev
        tsp = dma("sync", xfT_sp[2][:, D * t:D * (t + 1)], xfT[p][:], [tev])
        tok_xf3T[t] = tsp
        trk, tvs_last = router_softmax(t, tev, S2tm, cc_in[1])
        free[f"xfT{p}"] = [tsp, trk]

    t_we2 = []
    for k in range(KC):
        q = "sync" if k % 2 == 0 else "gpsimd"
        ws = [t_stash[k]]
        if k < 2:
            ws += [F("wreg0"), F("wreg1")]
        t_we2.append(dma(q, arena[:, D * k:D * (k + 1)],
                         we_bf[:, D * k:D * (k + 1)], ws))

    tcs2 = sexp_batch(S2tm, cc_in[1], tvs_last)
    bis2 = Bisect(1, tcs2)

    tok_z2w = z_phase(xfT_sp[2], tok_xf3T, z2_sp, t_we2, injector=bis2.emit_iter)
    tth2 = bis2.finish()
    ths2 = s_values(S2tm, s2c, hs2, tth2)

    tok_y = [None] * NT
    for t in range(NT):
        p = t % 2
        tx = dma("sync", x3rB[p][:], x3_sp[128 * t:128 * (t + 1), :],
                 [tok_x3w[t][7], F(f"x3rB{p}")])
        tz = dma("sync", zr[p][:], z2_sp[128 * t:128 * (t + 1), :],
                 [tok_z2w[t], F(f"zr{p}")])
        tv1 = P.ev("vector", lambda e, o=bigf[p][:], z=zr[p][:], h=hs2[:, t:t + 1], x=x3rB[p][:]:
                   e.scalar_tensor_tensor(out=o, in0=z, scalar=h, in1=x,
                                          op0=ALU.mult, op1=ALU.add),
                   [tx, tz, ths2, F(f"bigf{p}")], inc="vs")
        free[f"zr{p}"] = tv1
        free[f"x3rB{p}"] = tv1
        tv2 = P.ev("vector", lambda e, o=bigf[p][:], h=hs2[:, t:t + 1]:
                   e.scalar_tensor_tensor(out=o, in0=beb[:], scalar=h, in1=o,
                                          op0=ALU.mult, op1=ALU.add), [t_beb], inc="vs")
        ty = dma("sync", y_out[128 * t:128 * (t + 1), :], bigf[p][:], [tv2])
        tok_y[t] = ty
        free[f"bigf{p}"] = ty

    P.ev("sync", None, [tok_y[NT - 1]])

    with nc.Block() as block:
        def emitter(engname):
            def emit(eng):
                for ename, ws, fn, incs in P.events:
                    if ename != engname:
                        continue
                    for s, v in ws:
                        eng.wait_ge(sems[s], v)
                    if fn is None:
                        continue
                    ins = fn(eng)
                    for s, n in incs:
                        ins = ins.then_inc(sems[s], n)
            return emit
        block.sync(emitter("sync"))
        block.gpsimd(emitter("gpsimd"))
        block.vector(emitter("vector"))
        block.scalar(emitter("scalar"))
        block.tensor(emitter("tensor"))

    ctx.close()
    return nc


def kernel(L0_attn_x, L0_attn_Wq, L0_attn_bq, L0_attn_Wk, L0_attn_bk,
           L0_attn_Wv, L0_attn_bv, L0_attn_Wg, L0_attn_bg,
           L0_ffn_router_w, L0_ffn_router_b, L0_ffn_expert_w, L0_ffn_expert_b):
    x = np.ascontiguousarray(np.asarray(L0_attn_x, dtype=np.float32)).reshape(8, 4096, 2048)
    f32 = lambda a: np.ascontiguousarray(np.asarray(a, dtype=np.float32))
    consts = {
        "wq": f32(np.asarray(L0_attn_Wq).T), "wk": f32(np.asarray(L0_attn_Wk).T),
        "wv": f32(np.asarray(L0_attn_Wv).T),
        "wg": f32(np.asarray(L0_attn_Wg).T), "we": f32(L0_ffn_expert_w),
        "rw": f32(L0_ffn_router_w),
        "bq": f32(L0_attn_bq).reshape(1, 2048), "bk": f32(L0_attn_bk).reshape(1, 2048),
        "bv": f32(L0_attn_bv).reshape(1, 2048), "bg": f32(L0_attn_bg).reshape(1, 2048),
        "be": f32(L0_ffn_expert_b).reshape(1, 2048),
        "rb": f32(L0_ffn_router_b).reshape(1, 16),
        "id_bf": np.eye(128, dtype=ml_dtypes.bfloat16),
        "id_f": np.eye(128, dtype=np.float32),
        "tri": np.triu(np.ones((128, 128))).astype(ml_dtypes.bfloat16),
        "ones1": np.ones((1, 128), dtype=np.float32),
        "gmat": (np.arange(128)[:, None] % 16 ==
                 np.arange(128)[None, :] % 16).astype(np.float32),
    }
    nc = build_nc()
    in_maps = [dict(consts, x=np.ascontiguousarray(x[i])) for i in range(8)]
    res = run_bass_kernel_spmd(nc, in_maps, core_ids=list(range(8)))
    y = np.concatenate([res.results[i]["y"].reshape(2, 2048, 2048) for i in range(8)], axis=0)
    return y
